# version 20
# baseline (speedup 1.0000x reference)
import numpy as np
import ml_dtypes
import concourse.bass as bass
import concourse.mybir as mybir
from concourse.bass_utils import run_bass_kernel_spmd
from contextlib import ExitStack

F32 = mybir.dt.float32
BF16 = mybir.dt.bfloat16
I32 = mybir.dt.int32
I8 = mybir.dt.int8
AF = mybir.ActivationFunctionType
ALU = mybir.AluOpType
BF = ml_dtypes.bfloat16

H = 4096
HT = 32
II = 11008
IT = 86
NT = 32
S = 512
CH = 2
R = 32

ENGS = ("sync", "gpsimd", "tensor", "vector", "scalar")
CSEMS = ("mm", "ve", "ac")


def build_program():
    nc = bass.Bass("TRN2", target_bir_lowering=False)

    ap = lambda t: t.ap()
    xt_h = ap(nc.dram_tensor("xt", [H, CH * S], F32, kind="ExternalInput"))
    gq_h = ap(nc.dram_tensor("gq", [IT * 128, HT * 128], I8, kind="ExternalInput"))
    uq_h = ap(nc.dram_tensor("uq", [IT * 128, HT * 128], I8, kind="ExternalInput"))
    dq_h = ap(nc.dram_tensor("dq", [NT * 128, IT * 128], I8, kind="ExternalInput"))
    wt_h = ap(nc.dram_tensor("wt", [NT * 128, HT * 128], F32, kind="ExternalInput"))
    mk_h = ap(nc.dram_tensor("mk", [H, H], I8, kind="ExternalInput"))
    gscr_h = ap(nc.dram_tensor("gscr", [1, II], F32, kind="ExternalInput"))
    uscr_h = ap(nc.dram_tensor("uscr", [1, II], F32, kind="ExternalInput"))
    dscr_h = ap(nc.dram_tensor("dscr", [1, H], F32, kind="ExternalInput"))
    gscp_h = ap(nc.dram_tensor("gscp", [128, IT], F32, kind="ExternalInput"))
    uscp_h = ap(nc.dram_tensor("uscp", [128, IT], F32, kind="ExternalInput"))
    dscp_h = ap(nc.dram_tensor("dscp", [128, NT], F32, kind="ExternalInput"))
    gat_h = ap(nc.dram_tensor("gat", [128, HT * R], F32, kind="ExternalInput"))
    uat_h = ap(nc.dram_tensor("uat", [128, HT * R], F32, kind="ExternalInput"))
    dat_h = ap(nc.dram_tensor("dat", [128, IT * R], F32, kind="ExternalInput"))
    gb_h = ap(nc.dram_tensor("gb", [R, II], F32, kind="ExternalInput"))
    ub_h = ap(nc.dram_tensor("ub", [R, II], F32, kind="ExternalInput"))
    db_h = ap(nc.dram_tensor("db", [R, H], F32, kind="ExternalInput"))
    bias_h = ap(nc.dram_tensor("biasp", [128, NT], F32, kind="ExternalInput"))
    out_h = ap(nc.dram_tensor("out", [H, CH * S], F32, kind="ExternalOutput"))
    bpg_h = ap(nc.dram_tensor("bpg_s", [IT * R, 128], BF16, kind="Internal"))
    bpu_h = ap(nc.dram_tensor("bpu_s", [IT * R, 128], BF16, kind="Internal"))
    bpd_h = ap(nc.dram_tensor("bpd_s", [NT * R, 128], BF16, kind="Internal"))
    wms_h = ap(nc.dram_tensor("wm_s", [NT * 128, HT * 128], BF16, kind="Internal"))

    p_idx = np.arange(128)
    t_idx = np.arange(32)
    rexp = (p_idx[:, None, None] == (4 * t_idx[None, :, None] + (p_idx[None, None, :] // 32)))
    rexp_c = ap(nc.inline_tensor(np.ascontiguousarray(rexp.reshape(128, 4096).astype(BF)), "rexp_c"))
    i128_c = ap(nc.inline_tensor(np.eye(128, dtype=BF), "i128_c"))
    c75_c = ap(nc.inline_tensor(np.full((1, 128), -7.5, BF), "c75_c"))
    onesc_c = ap(nc.inline_tensor(np.ones((128, 1), BF), "onesc_c"))
    ones1_c = ap(nc.inline_tensor(np.ones((1, 128), np.float32), "ones1_c"))
    ones32_c = ap(nc.inline_tensor(np.ones((1, 32), np.float32), "ones32_c"))
    eps_c = ap(nc.inline_tensor(np.full((1, 1), 1e-5, np.float32), "eps_c"))
    lnc_c = ap(nc.inline_tensor(np.full((1, 1), np.log(np.float32(0.02)), np.float32),
                                "lnc_c"))

    BASE = 16448

    def sb(name, shape, dtype, off):
        return nc.alloc_sbuf_tensor_at(name, shape, dtype, offset=BASE + off)

    xn_t = sb("xn", [128, HT * S], BF16, 0)
    h_t = sb("h", [128, IT * S], BF16, 32768)
    s_t = sb("sres", [128, NT * S], BF16, 120832)
    AR = 153600
    agv = sb("agv", [128, HT * R], BF16, AR)
    auv = sb("auv", [128, HT * R], BF16, AR + 2048)
    wgv = [sb("wgv0", [128, HT * 128], BF16, AR),
           sb("wgv1", [128, HT * 128], BF16, AR + 8192)]
    wuv = [sb("wuv0", [128, HT * 128], BF16, AR + 16384),
           sb("wuv1", [128, HT * 128], BF16, AR + 24576)]
    wav = [sb("wav0", [128, HT * 128], BF16, AR),
           sb("wav1", [128, HT * 128], BF16, AR + 8192)]
    wdv = [sb("wdv0", [128, IT * 128], BF16, AR),
           sb("wdv1", [128, IT * 128], BF16, 0)]
    wdv_s = sb("wdv_s", [128, IT * 128], BF16, 120832)
    a_g = sb("a_g", [128, IT], F32, 186368)
    a_u = sb("a_u", [128, IT], F32, 186720)
    a_d = sb("a_d", [128, NT], F32, 187072)
    biasp = sb("biasv", [128, NT], F32, 187200)
    t_ar = sb("t_ar", [128, 3 * S], BF16, 187328)
    bp_bb = sb("bp_bb", [128, 768], BF16, 190400)
    adb = sb("adb", [128, 66], BF16, 191936)
    m_sb = sb("m_sb", [1, S], F32, 192256)
    inv_sb = sb("inv_sb", [1, S], F32, 194304)
    c75_sb = sb("c75_sb", [1, 128], BF16, 196352)
    onesc_sb = sb("onesc_sb", [128, 1], BF16, 196608)
    ones1_sb = sb("ones1_sb", [1, 128], F32, 196672)
    ones32_sb = sb("ones32_sb", [1, 32], F32, 197184)
    xst = sb("xst", [128, 2 * S], F32, 197312)
    ost = sb("ost", [128, 2 * S], F32, 201408)
    sgst = sb("sgst", [128, 2 * S], BF16, 207552)
    eps_sb = sb("eps_sb", [1, 1], F32, 209600)
    lnc_sb = sb("lnc_sb", [1, 1], F32, 209632)
    SR = 120832
    gbf_sb = sb("gbf_sb", [R, II], F32, 32768)
    iar_sb = sb("iar_sb", [1, II], F32, 76800)
    bst_sb = sb("bst_sb", [R, 2 * S], BF16, 205504)
    wf32 = sb("wf32", [128, 4096], F32, SR)
    rexpf = sb("rexpf", [128, 4096], BF16, SR)
    accm = sb("accm", [128, 4096], I8, SR + 16384)
    bmq_t = sb("bmq", [128, NT * 128], BF16, SR + 16384)
    wtmp = sb("wtmp", [128, 4096], BF16, SR + 24576)
    bm01_sb = sb("bm01_sb", [128, 128], BF16, 209664)
    bmn_sb = sb("bmn_sb", [128, 128], BF16, 209920)
    i128b = sb("i128b", [128, 128], BF16, 210176)

    var_v = ost[0:1, 0:S]

    ps = [nc.alloc_psum_tensor(f"ps{k}", [128, S], F32) for k in range(8)]

    sch = {k: [] for k in ENGS}
    cnt = dict.fromkeys(CSEMS, 0)
    stream_cnt = {}
    stream_eng = {}
    sems = {}

    def em(eng, f):
        sch[eng].append(f)

    sat = {}

    def wait(eng, k, v):
        if v and v > 0 and sat.get((eng, k), 0) < v:
            sat[(eng, k)] = v
            em(eng, lambda e, k=k, v=v: e.wait_ge(sems[k], v))

    def waits(eng, ws):
        d = {}
        for k, v in ws:
            if v and v > 0:
                d[k] = max(d.get(k, 0), v)
        for k, v in d.items():
            wait(eng, k, v)

    def dma(eng, out_ap, in_ap, st, accum=False):
        assert stream_eng.setdefault(st, eng) == eng, st
        prev = stream_cnt.get(st, 0)
        if prev >= 32:
            em(eng, lambda e, st=st, v=prev - 16: e.wait_ge(sems[st], v))
        stream_cnt[st] = prev + 16
        v = stream_cnt[st]
        if accum:
            em(eng, lambda e, o=out_ap, i=in_ap, st=st:
               e.dma_start(out=o, in_=i, accum_op=ALU.add).then_inc(sems[st], 16))
        else:
            em(eng, lambda e, o=out_ap, i=in_ap, st=st:
               e.dma_start(out=o, in_=i).then_inc(sems[st], 16))
        return (st, v)

    def mm(out_ap, lhsT, rhs, start, stop, inc=False):
        if inc:
            cnt["mm"] += 1
            v = cnt["mm"]
            em("tensor", lambda e, o=out_ap, l=lhsT, r=rhs, st=start, sp=stop, v=v:
               e.matmul(out=o, lhsT=l, rhs=r, start=st, stop=sp).then_inc(sems["mm"], 1))
            return v
        em("tensor", lambda e, o=out_ap, l=lhsT, r=rhs, st=start, sp=stop:
           e.matmul(out=o, lhsT=l, rhs=r, start=st, stop=sp))
        return None

    def act(fn_builder, inc=True):
        if inc:
            cnt["ac"] += 1
            v = cnt["ac"]
            em("scalar", lambda e, f=fn_builder, v=v: f(e).then_inc(sems["ac"], 1))
            return v
        em("scalar", lambda e, f=fn_builder: f(e))
        return None

    def vec(fn_builder, inc=True):
        if inc:
            cnt["ve"] += 1
            v = cnt["ve"]
            em("vector", lambda e, f=fn_builder, v=v: f(e).then_inc(sems["ve"], 1))
            return v
        em("vector", lambda e, f=fn_builder: f(e))
        return None

    bank_rd = [[] for _ in range(8)]

    def bank_write(bk):
        ws, bank_rd[bk] = bank_rd[bk], []
        waits("tensor", ws)

    res_rd = {}

    def res_write(eng, key):
        waits(eng, res_rd.pop(key, []))

    def res_read(key, k, v):
        res_rd.setdefault(key, []).append((k, v))

    for dst, src in ((c75_sb[:], c75_c[:]),
                     (onesc_sb[:], onesc_c[:]), (ones1_sb[:], ones1_c[:]),
                     (ones32_sb[:], ones32_c[:]), (eps_sb[:], eps_c[:]),
                     (lnc_sb[:], lnc_c[:]),
                     (biasp[:], bias_h[:])):
        dma("sync", dst, src, "cst")
    dma("sync", a_g[:], gscp_h[:], "cst")
    dma("sync", a_u[:], uscp_h[:], "cst")
    m_ad = dma("sync", a_d[:], dscp_h[:], "cst")
    consts_mark = m_ad
    waits("scalar", [m_ad])
    act(lambda e: e.activation(out=a_g[:], in_=a_g[:], func=AF.Identity, scale=0.005))
    act(lambda e: e.activation(out=a_u[:], in_=a_u[:], func=AF.Identity, scale=0.005))
    act(lambda e: e.activation(out=a_d[:], in_=a_d[:], func=AF.Identity, scale=0.005))
    waits("vector", [consts_mark])
    for slot in range(6):
        vec(lambda e, slot=slot: e.tensor_copy(
            out=bp_bb[32:33, slot * 128:(slot + 1) * 128], in_=c75_sb[:]), inc=False)
    vec(lambda e: e.tensor_copy(out=adb[:, 32:33], in_=onesc_sb[:]), inc=False)
    vec(lambda e: e.tensor_copy(out=adb[:, 65:66], in_=onesc_sb[:]))

    waits("tensor", [consts_mark])

    def prep_bprime(scr_row, b_full, dst, n_out, stores):
        noc = (n_out + S - 1) // S
        state = {"pend": None}

        def flush():
            if state["pend"] is None:
                return
            oc, cw, bk, mmv, mb = state["pend"]
            state["pend"] = None
            waits("vector", [("mm", mmv), mb])
            res_write("vector", ("bst", bk))
            vv = vec(lambda e, oc=oc, cw=cw, bk=bk: e.tensor_tensor(
                out=bst_sb[0:R, bk * S:bk * S + cw],
                in0=gbf_sb[0:R, oc * S:oc * S + cw],
                in1=ps[bk][0:R, 0:cw], op=ALU.mult))
            bank_rd[bk].append(("ve", vv))
            res_read(("gbf",), "ve", vv)
            wait("sync", "ve", vv)
            nblk = cw // 128
            dst_ap = dst[oc * 4 * R:(oc * 4 + nblk) * R, :].rearrange(
                "(ot r) o -> r ot o", r=R)
            src_ap = bst_sb[0:R, bk * S:bk * S + cw].rearrange(
                "r (ot o) -> r ot o", o=128)
            sv = dma("sync", dst_ap, src_ap, "cst")
            res_read(("bst", bk), *sv)
            stores.append(sv)

        def step(oc):
            if oc == noc:
                flush()
                return
            if oc == 0:
                res_write("sync", ("iar",))
                state["m1"] = dma("sync", iar_sb[0:1, 0:n_out], scr_row, "cst")
                res_write("sync", ("gbf",))
                state["mb"] = dma("sync", gbf_sb[:, 0:n_out], b_full, "cst")
            m1, mb = state["m1"], state["mb"]
            cw = min(S, n_out - oc * S)
            bk = oc % 2
            row = iar_sb[0:1, oc * S:oc * S + cw]
            waits("scalar", [m1])
            act(lambda e, row=row: e.activation(out=row, in_=row,
                                                func=AF.Ln, scale=0.005), inc=False)
            v4 = act(lambda e, row=row: e.activation(out=row, in_=row, func=AF.Exp,
                                                     scale=-1.0, bias=lnc_sb[:]))
            wait("vector", "ac", v4)
            vmin = vec(lambda e, row=row: e.tensor_scalar_min(out=row, in0=row,
                                                              scalar1=1e30))
            bank_write(bk)
            waits("tensor", [("ve", vmin), mb])
            mmv = mm(ps[bk][0:R, 0:cw], ones32_sb[:], row, True, True, inc=True)
            res_read(("iar",), "mm", mmv)
            flush()
            state["pend"] = (oc, cw, bk, mmv, mb)

        return [lambda oc=oc: step(oc) for oc in range(noc + 1)]

    g_stores = []
    u_stores = []
    d_stores = []
    bsteps = (prep_bprime(gscr_h[:], gb_h[:], bpg_h, II, g_stores)
              + prep_bprime(uscr_h[:], ub_h[:], bpu_h, II, u_stores)
              + prep_bprime(dscr_h[:], db_h[:], bpd_h, H, d_stores))

    mkv = mk_h[:, :].rearrange("(nb r) k -> r nb k", r=32)
    ch_prev = {}

    def chain_issue(i):
        r, c = i // 4, i % 4
        st = f"ma{c}"
        if c in ch_prev:
            wait("gpsimd", *ch_prev[c])
        cs = slice(c * 1024, (c + 1) * 1024)
        ch_prev[c] = dma("gpsimd", accm[:, cs], mkv[r][:, cs], st, accum=(r > 0))

    xbl = {}
    for ht in range(2):
        xbl[ht] = dma("gpsimd", xn_t[:, ht * S:(ht + 1) * S],
                      xt_h[ht * 128:(ht + 1) * 128, 0:S], f"xb{ht}")
    pre_gu = {
        "wg1": dma("gpsimd", wgv[1][:], gq_h[128:256, :], "wg1"),
        "wu0": dma("gpsimd", wuv[0][:], uq_h[0:128, :], "wu0"),
        "wu1": dma("gpsimd", wuv[1][:], uq_h[128:256, :], "wu1"),
        "ad0": dma("gpsimd", adb[:, 0:32], dat_h[:, 0:R], "ad0"),
        "ad1": dma("gpsimd", adb[:, 33:65], dat_h[:, R:2 * R], "ad1"),
    }
    for ht in range(2, HT):
        xbl[ht] = dma("gpsimd", xn_t[:, ht * S:(ht + 1) * S],
                      xt_h[ht * 128:(ht + 1) * 128, 0:S], f"xb{ht % 8}")
    i128_ld = dma("sync", i128b[:], i128_c[:], "cst")
    rexp_ld = dma("sync", rexpf[:], rexp_c[:], "cst")
    for i in range(128):
        chain_issue(i)
    mload = {}
    wmst = {}
    wm_done = [None, None]
    bmark = {}

    def burst_step(ot):
        if ot < 8:
            pass
        elif ot == 8:
            waits("vector", [ch_prev[c] for c in range(4)])

            def _red(e):
                with nc.allow_low_precision(reason="0/1 mask any-nonzero; sign-exact"):
                    return e.tensor_reduce(
                        out=bm01_sb[:], in_=accm[:].rearrange("p (kb k) -> p kb k", k=32),
                        axis=mybir.AxisListType.X, op=ALU.add)

            vred = vec(_red)
            res_read(("accm",), "ve", vred)
            wait("vector", "ve", vred)
            bmark["vbm"] = vec(lambda e: e.tensor_single_scalar(
                out=bm01_sb[:], in_=bm01_sb[:], scalar=0.0, op=ALU.is_gt))
        elif ot == 9:
            bank_write(6)
            waits("tensor", [("ve", bmark["vbm"]), i128_ld])
            mbn = mm(ps[6][:, 0:128], bm01_sb[:], i128b[:], True, True, inc=True)
            waits("scalar", [("mm", mbn)])
            bmark["abn"] = act(lambda e: e.activation(out=bmn_sb[:], in_=ps[6][:, 0:128],
                                                      func=AF.Copy))
            bank_rd[6].append(("ac", bmark["abn"]))
        elif 10 <= ot < 26:
            for j in range(2):
                kt = 2 * (ot - 10) + j
                bk = 6 + kt % 2
                bank_write(bk)
                waits("tensor", [("ac", bmark["abn"])] + ([rexp_ld] if kt == 0 else []))
                mq = mm(ps[bk][:, 0:128], rexpf[:, kt * 128:(kt + 1) * 128],
                        bmn_sb[:], True, True, inc=True)
                res_read(("rexpf",), "mm", mq)
                waits("scalar", [("mm", mq)])
                if kt == 0:
                    res_write("scalar", ("accm",))
                aq = act(lambda e, kt=kt, bk=bk: e.activation(
                    out=bmq_t[:, kt * 128:(kt + 1) * 128], in_=ps[bk][:, 0:128],
                    func=AF.Copy))
                bank_rd[bk].append(("ac", aq))
                res_read(("bmq",), "ac", aq)
                if kt == NT - 1:
                    wm_done[0] = ("ac", cnt["ac"])
        elif ot >= 26 and bmark.get("wc", 0) < NT and (ot % 2 == 0 or ot >= 81):
            t = bmark.get("wc", 0)
            bmark["wc"] = t + 1
            wstep(t)

    def wstep(t):
        if t == 0:
            res_write("scalar", ("rexpf",))
            wml = dma("scalar", wf32[:], wt_h[0:128, :], "wf0")
            mload[NT] = wml
        else:
            wml = mload[NT + t]
        waits("vector", [wml, wm_done[0]])
        res_write("vector", ("wtmp",))
        vec(lambda e: e.tensor_copy(out=wtmp[:], in_=wf32[:]), inc=False)
        res_read(("wf32",), "ve", cnt["ve"] + 1)
        if t + 1 < NT:
            eng2 = "scalar" if (t + 1) % 2 == 0 else "sync"
            res_write(eng2, ("wf32",))
            mload[NT + t + 1] = dma(eng2, wf32[:],
                                    wt_h[(t + 1) * 128:(t + 2) * 128, :],
                                    f"wf{(t + 1) % 2}")
        bmq_v = bmq_t[:].rearrange("p (kt nbg) -> p kt nbg", kt=NT)[
            :, :, 4 * t:4 * t + 4].unsqueeze(3).broadcast_to([128, NT, 4, 32])
        wmv = vec(lambda e, bmq_v=bmq_v: e.tensor_tensor(
            out=wtmp[:].rearrange("p (kt nb r) -> p kt nb r", kt=NT, nb=4),
            in0=wtmp[:].rearrange("p (kt nb r) -> p kt nb r", kt=NT, nb=4),
            in1=bmq_v, op=ALU.mult))
        res_read(("bmq",), "ve", wmv)
        wait("sync", "ve", wmv)
        wmst[t] = dma("sync", wms_h[t * 128:(t + 1) * 128, :], wtmp[:], "cst")
        res_read(("wtmp",), *wmst[t])
        if t == NT - 1:
            wm_done[1] = wmst[t]

    prev_md_last = None
    t_pre = {}
    for u in range(CH):
        tok = slice(u * S, (u + 1) * S)

        xq_mark = {}
        for ht in range(HT):
            par = ht % 2
            waits("scalar", [xbl[ht]])
            res_write("scalar", ("sgst", par))
            xq_mark[ht] = act(lambda e, ht=ht, par=par: e.activation(
                out=sgst[:, par * S:(par + 1) * S], in_=xn_t[:, ht * S:(ht + 1) * S],
                func=AF.Square))
            if ht == 0:
                bank_write(6)
                bank_write(7)
            waits("tensor", [("ac", xq_mark[ht]), xbl[ht]])
            mm(ps[6][0:1, :], onesc_sb[:], xn_t[:, ht * S:(ht + 1) * S],
               start=(ht == 0), stop=(ht == HT - 1), inc=False)
            mm(ps[7][0:1, :], onesc_sb[:], sgst[:, par * S:(par + 1) * S],
               start=(ht == 0), stop=(ht == HT - 1), inc=True)
            ht_mark = cnt["mm"]
            res_read(("sgst", par), "mm", ht_mark)
            if u == 0:
                if bsteps:
                    bsteps.pop(0)()
                if bsteps:
                    bsteps.pop(0)()
        stats_mark = cnt["mm"]
        waits("scalar", [("mm", stats_mark)])
        am = act(lambda e: e.activation(out=m_sb[:], in_=ps[6][0:1, :],
                                        func=AF.Identity, scale=1.0 / H))
        bank_rd[6].append(("ac", am))
        waits("vector", [("ac", am)])
        vec(lambda e: e.tensor_tensor(out=inv_sb[:], in0=m_sb[:], in1=m_sb[:],
                                      op=ALU.mult), inc=False)
        waits("vector", [("mm", stats_mark)])
        res_write("vector", ("ost", 0))
        vvar = vec(lambda e: e.scalar_tensor_tensor(
            out=var_v, in0=ps[7][0:1, :], scalar=1.0 / H, in1=inv_sb[:],
            op0=ALU.mult, op1=ALU.subtract))
        bank_rd[7].append(("ve", vvar))
        waits("scalar", [("ve", vvar)])
        act(lambda e: e.activation(out=var_v, in_=var_v, func=AF.Ln, bias=eps_sb[:]),
            inc=False)
        ainv = act(lambda e: e.activation(out=inv_sb[:], in_=var_v, func=AF.Exp,
                                          scale=-0.5))
        bank_write(2)
        waits("tensor", [("ac", am)])
        mm(ps[2][:, :], ones1_sb[:], m_sb[:], True, True, inc=False)
        bank_write(3)
        waits("tensor", [("ac", ainv)])
        mib = mm(ps[3][:, :], ones1_sb[:], inv_sb[:], True, True, inc=True)
        res_read(("ost", 0), "mm", mib)
        xn_mark = {}
        for ht in range(HT):
            par = ht % 2
            if ht == 0:
                waits("vector", [("mm", mib)])
                res_write("vector", ("ost", 0))
                res_write("vector", ("ost", 1))
            vec(lambda e, par=par, ht=ht: e.tensor_tensor(
                out=ost[:, par * S:(par + 1) * S], in0=xn_t[:, ht * S:(ht + 1) * S],
                in1=ps[2][:, :], op=ALU.subtract), inc=False)
            xn_mark[ht] = vec(lambda e, par=par, ht=ht: e.tensor_tensor(
                out=xn_t[:, ht * S:(ht + 1) * S], in0=ost[:, par * S:(par + 1) * S],
                in1=ps[3][:, :], op=ALU.mult))
            res_read(("ost", par), "ve", xn_mark[ht])
            if ht == 0:
                bank_write(6)
            waits("tensor", [("ve", xn_mark[ht])])
            mm(ps[6][0:1, :], onesc_sb[:], xn_t[:, ht * S:(ht + 1) * S],
               start=(ht == 0), stop=(ht == HT - 1), inc=(ht == HT - 1))
            if u == 0:
                if bsteps:
                    bsteps.pop(0)()
        rx_group = cnt["mm"]
        bank_rd[2].append(("ve", xn_mark[HT - 1]))
        bank_rd[3].append(("ve", xn_mark[HT - 1]))
        waits("scalar", [("mm", rx_group)])
        act(lambda e: e.activation(out=t_ar[32:33, 0:S], in_=ps[6][0:1, :],
                                   func=AF.Copy), inc=False)
        arx = act(lambda e: e.activation(out=t_ar[32:33, S:2 * S], in_=ps[6][0:1, :],
                                         func=AF.Copy))
        bank_rd[6].append(("ac", arx))
        if u == 0:
            while bsteps:
                bsteps.pop(0)()

        res_write("gpsimd", ("arena",))
        if u == 0:
            tga = dma("gpsimd", agv[:], gat_h[:], "wg0")
            tua = dma("gpsimd", auv[:], uat_h[:], "wu0")
        else:
            tga, tua = t_pre["g"], t_pre["u"]
        bank_write(4)
        waits("tensor", [tga])
        for ht in range(HT):
            mm(ps[4][0:R, :], agv[:, ht * R:(ht + 1) * R], xn_t[:, ht * S:(ht + 1) * S],
               start=(ht == 0), stop=(ht == HT - 1), inc=(ht == HT - 1))
        tg_mark = cnt["mm"]
        bank_write(5)
        waits("tensor", [tua])
        for ht in range(HT):
            mm(ps[5][0:R, :], auv[:, ht * R:(ht + 1) * R], xn_t[:, ht * S:(ht + 1) * S],
               start=(ht == 0), stop=(ht == HT - 1), inc=(ht == HT - 1))
        tu_mark = cnt["mm"]
        waits("scalar", [("mm", tg_mark)])
        atg = act(lambda e: e.activation(out=t_ar[0:R, 0:S], in_=ps[4][0:R, :], func=AF.Copy))
        bank_rd[4].append(("ac", atg))
        waits("scalar", [("mm", tu_mark)])
        atu = act(lambda e: e.activation(out=t_ar[0:R, S:2 * S], in_=ps[5][0:R, :], func=AF.Copy))
        bank_rd[5].append(("ac", atu))

        wg_mark = {}
        wu_mark = {}
        bp_mark = {}
        ad_mark = {}

        def load_gu(ot, first=False):
            par = ot % 2
            if first:
                waits("gpsimd", [("mm", tu_mark)])
            res_write("gpsimd", ("wg", par))
            wg_mark[ot] = dma("gpsimd", wgv[par][:],
                              gq_h[ot * 128:(ot + 1) * 128, :], f"wg{par}")
            res_write("gpsimd", ("wu", par))
            wu_mark[ot] = dma("gpsimd", wuv[par][:],
                              uq_h[ot * 128:(ot + 1) * 128, :], f"wu{par}")

        def load_bp(ot):
            par = ot % 2
            res_write("sync", ("bpg", par))
            waits("sync", [g_stores[min(ot // 4, len(g_stores) - 1)]])
            dma("sync", bp_bb[0:R, par * 128:(par + 1) * 128],
                bpg_h[ot * R:(ot + 1) * R, :], f"bp{par}")
            res_write("sync", ("bpu", par))
            waits("sync", [u_stores[min(ot // 4, len(u_stores) - 1)]])
            bp_mark[ot] = dma("sync", bp_bb[0:R, 256 + par * 128:256 + (par + 1) * 128],
                              bpu_h[ot * R:(ot + 1) * R, :], f"bp{par}")

        def load_ad(ot):
            par = ot % 2
            res_write("gpsimd", ("adb", par))
            ad_mark[ot] = dma("gpsimd", adb[:, par * 33:par * 33 + 32],
                              dat_h[:, ot * R:(ot + 1) * R], f"ad{par}")

        if u == 0:
            waits("gpsimd", [("mm", tu_mark)])
            res_write("gpsimd", ("wg", 0))
            wg_mark[0] = dma("gpsimd", wgv[0][:], gq_h[0:128, :], "wg0")
            wg_mark[1] = pre_gu["wg1"]
            wu_mark[0] = pre_gu["wu0"]
            wu_mark[1] = pre_gu["wu1"]
            ad_mark[0] = pre_gu["ad0"]
            ad_mark[1] = pre_gu["ad1"]
        else:
            load_gu(0, first=True)
            load_gu(1, first=True)
            load_ad(0)
            load_ad(1)
        load_bp(0)
        load_bp(1)
        mg_mark = {}
        mu_mark = {}
        mtd_mark = {}
        h_mark = {}
        sg_mark = {}
        for ot in range(IT):
            par = ot % 2
            gbk, ubk = par, 2 + par
            bank_write(gbk)
            waits("tensor", [("ac", arx), bp_mark[ot], ("ac", atg)])
            mm(ps[gbk][:, :], bp_bb[0:33, par * 128:(par + 1) * 128],
               t_ar[0:33, 0:S], True, False)
            waits("tensor", [wg_mark[ot]])
            for ht in range(HT):
                mm(ps[gbk][:, :], wgv[par][:, ht * 128:(ht + 1) * 128],
                   xn_t[:, ht * S:(ht + 1) * S], False, (ht == HT - 1), inc=(ht == HT - 1))
            mg_mark[ot] = cnt["mm"]
            res_read(("wg", par), "mm", mg_mark[ot])
            res_read(("bpg", par), "mm", mg_mark[ot])
            res_read(("bpu", par), "mm", mg_mark[ot])
            bank_write(ubk)
            waits("tensor", [("ac", atu)])
            mm(ps[ubk][:, :], bp_bb[0:33, 256 + par * 128:256 + (par + 1) * 128],
               t_ar[0:33, S:2 * S], True, False)
            waits("tensor", [wu_mark[ot]])
            for ht in range(HT):
                mm(ps[ubk][:, :], wuv[par][:, ht * 128:(ht + 1) * 128],
                   xn_t[:, ht * S:(ht + 1) * S], False, (ht == HT - 1), inc=(ht == HT - 1))
            mu_mark[ot] = cnt["mm"]
            res_read(("wu", par), "mm", mu_mark[ot])
            if ot > 0:
                po = ot - 1
                ppar = po % 2
                if po == 0:
                    bank_write(4)
                    waits("tensor", [("ac", atg)])
                waits("tensor", [("ve", h_mark[po]), ad_mark[po]])
                mm(ps[4][0:33, :], adb[:, ppar * 33:(ppar + 1) * 33],
                   h_t[:, po * S:(po + 1) * S], start=(po == 0), stop=(po == IT - 1),
                   inc=True)
                mtd_mark[po] = cnt["mm"]
                res_read(("adb", ppar), "mm", mtd_mark[po])
                if po + 2 < IT:
                    load_ad(po + 2)
            waits("scalar", [("mm", mg_mark[ot])])
            res_write("scalar", ("sgst", par))
            sg_mark[ot] = act(lambda e, ot=ot, par=par, gbk=gbk: e.activation(
                out=sgst[:, par * S:(par + 1) * S], in_=ps[gbk][:, :], func=AF.Silu,
                scale=a_g[:, ot:ot + 1]))
            bank_rd[gbk].append(("ac", sg_mark[ot]))
            waits("vector", [("ac", sg_mark[ot]), ("mm", mu_mark[ot])])
            h_mark[ot] = vec(lambda e, ot=ot, par=par, ubk=ubk: e.scalar_tensor_tensor(
                out=h_t[:, ot * S:(ot + 1) * S], in0=ps[ubk][:, :],
                scalar=a_u[:, ot:ot + 1], in1=sgst[:, par * S:(par + 1) * S],
                op0=ALU.mult, op1=ALU.mult))
            bank_rd[ubk].append(("ve", h_mark[ot]))
            res_read(("sgst", par), "ve", h_mark[ot])
            if ot + 2 < IT:
                load_bp(ot + 2)
                load_gu(ot + 2)
            if u == 0:
                burst_step(ot)
        po = IT - 1
        ppar = po % 2
        waits("tensor", [("ve", h_mark[po]), ad_mark[po]])
        mm(ps[4][0:33, :], adb[:, ppar * 33:(ppar + 1) * 33],
           h_t[:, po * S:(po + 1) * S], start=False, stop=True, inc=True)
        mtd_mark[po] = cnt["mm"]
        res_read(("adb", ppar), "mm", mtd_mark[po])
        waits("scalar", [("mm", mtd_mark[po])])
        atd = act(lambda e: e.activation(out=t_ar[0:33, 2 * S:3 * S], in_=ps[4][0:33, :],
                                         func=AF.Copy))
        bank_rd[4].append(("ac", atd))

        wa_mark = {}
        ms_mark = {}

        def load_wa(nt, extra=None):
            par = nt % 2
            eng = "sync" if par == 0 else "scalar"
            res_write(eng, ("wa", par))
            if extra:
                waits(eng, extra)
            if extra is None and u == 0:
                waits(eng, [wmst[nt]])
            wa_mark[nt] = dma(eng, wav[par][:],
                              wms_h[nt * 128:(nt + 1) * 128, :], f"wa{par}")

        wmg = [wmst[0]] if u == 0 else []
        load_wa(0, extra=[("mm", mg_mark[IT - 2])] + ([wmst[0]] if u == 0 else []))
        load_wa(1, extra=[("mm", mg_mark[IT - 1])] + ([wmst[1]] if u == 0 else []))
        for nt in range(NT):
            par = nt % 2
            sbk = par
            bank_write(sbk)
            waits("tensor", [wa_mark[nt]])
            for kt in range(HT):
                mm(ps[sbk][:, :], wav[par][:, kt * 128:(kt + 1) * 128],
                   xn_t[:, kt * S:(kt + 1) * S], start=(kt == 0), stop=(kt == HT - 1),
                   inc=(kt == HT - 1))
            ms_mark[nt] = cnt["mm"]
            res_read(("wa", par), "mm", ms_mark[nt])
            waits("scalar", [("mm", ms_mark[nt])])
            if u == 0 and nt == 0:
                waits("scalar", [wm_done[1], ("ac", wm_done[0][1])])
                for key in (("bmq",), ("wf32",), ("wtmp",), ("accm",), ("rexpf",)):
                    res_write("scalar", key)
            se = act(lambda e, nt=nt, sbk=sbk: e.activation(
                out=s_t[:, nt * S:(nt + 1) * S], in_=ps[sbk][:, :], func=AF.Identity,
                bias=biasp[:, nt:nt + 1]))
            bank_rd[sbk].append(("ac", se))
            if nt + 2 < NT:
                load_wa(nt + 2)

        wd_mark = {}
        md_mark = {}
        bpd_mark = {}
        gt_mark = {}
        cb_mark = {}
        xld2 = {}

        def wd_buf(ho):
            if ho % 2 == 1 and ho >= 25:
                return wdv_s, ("wds",)
            return wdv[ho % 2], ("wd", ho % 2)

        def load_wd(ho, extra=None):
            buf, key = wd_buf(ho)
            res_write("gpsimd", key)
            if ho == 25:
                waits("gpsimd", [("ve", cb_mark[21])])
            if extra:
                waits("gpsimd", extra)
            wd_mark[ho] = dma("gpsimd", buf[:],
                              dq_h[ho * 128:(ho + 1) * 128, :], f"wd{ho % 2}l")

        def load_bpd(ho):
            par = ho % 2
            res_write("sync", ("bpd", par))
            waits("sync", [d_stores[min(ho // 4, len(d_stores) - 1)]])
            bpd_mark[ho] = dma("sync", bp_bb[0:R, 512 + par * 128:512 + (par + 1) * 128],
                               bpd_h[ho * R:(ho + 1) * R, :], f"bp{par}")

        def load_x2(ho):
            par = ho % 2
            res_write("scalar", ("xst", par))
            xld2[ho] = dma("scalar", xst[:, par * S:(par + 1) * S],
                           xt_h[ho * 128:(ho + 1) * 128, tok], f"x{par}")

        load_wd(0, extra=[("mm", ms_mark[NT - 1])])
        load_wd(1, extra=[("mm", ms_mark[NT - 1])])
        load_bpd(0)
        load_bpd(1)
        load_x2(0)
        load_x2(1)
        for ho in range(NT):
            par = ho % 2
            dbk = 2 + par
            bank_write(dbk)
            waits("tensor", [("ac", atd), bpd_mark[ho]])
            mm(ps[dbk][:, :], bp_bb[0:33, 512 + par * 128:512 + (par + 1) * 128],
               t_ar[0:33, 2 * S:3 * S], True, False)
            buf, key = wd_buf(ho)
            waits("tensor", [wd_mark[ho]])
            for it in range(IT):
                mm(ps[dbk][:, :], buf[:, it * 128:(it + 1) * 128],
                   h_t[:, it * S:(it + 1) * S], False, (it == IT - 1), inc=(it == IT - 1))
            md_mark[ho] = cnt["mm"]
            res_read(key, "mm", md_mark[ho])
            res_read(("bpd", par), "mm", md_mark[ho])
            waits("scalar", [("mm", md_mark[ho])])
            res_write("scalar", ("sgst", par))
            gt_mark[ho] = act(lambda e, ho=ho, par=par, dbk=dbk: e.activation(
                out=sgst[:, par * S:(par + 1) * S], in_=ps[dbk][:, :], func=AF.Silu,
                scale=a_d[:, ho:ho + 1]))
            bank_rd[dbk].append(("ac", gt_mark[ho]))
            waits("vector", [("ac", gt_mark[ho]), xld2[ho]])
            res_write("vector", ("ost", par))
            vec(lambda e, ho=ho, par=par: e.tensor_tensor(
                out=ost[:, par * S:(par + 1) * S], in0=s_t[:, ho * S:(ho + 1) * S],
                in1=sgst[:, par * S:(par + 1) * S], op=ALU.mult), inc=False)
            cb_mark[ho] = vec(lambda e, ho=ho, par=par: e.tensor_tensor(
                out=ost[:, par * S:(par + 1) * S], in0=ost[:, par * S:(par + 1) * S],
                in1=xst[:, par * S:(par + 1) * S], op=ALU.add))
            res_read(("sgst", par), "ve", cb_mark[ho])
            res_read(("xst", par), "ve", cb_mark[ho])
            waits("sync", [("ve", cb_mark[ho])])
            ov = dma("sync", out_h[ho * 128:(ho + 1) * 128, tok],
                     ost[:, par * S:(par + 1) * S], f"o{par}")
            res_read(("ost", par), *ov)
            if ho + 2 < NT:
                load_bpd(ho + 2)
                load_x2(ho + 2)
                load_wd(ho + 2)
            if u == 0 and ho % 2 == 1 and 23 <= ho <= 29:
                if ho == 23:
                    waits("gpsimd", [("mm", md_mark[23])])
                for ht2 in range(8 * ((ho - 23) // 2), 8 * ((ho - 23) // 2) + 8):
                    xbl[ht2] = dma("gpsimd", xn_t[:, ht2 * S:(ht2 + 1) * S],
                                   xt_h[ht2 * 128:(ht2 + 1) * 128, 512:1024],
                                   f"xb{ht2 % 4}")
            if u == 0 and ho == NT - 1:
                waits("gpsimd", [("mm", md_mark[NT - 2])])
                t_pre["g"] = dma("gpsimd", agv[:], gat_h[:], "wg0")
                t_pre["u"] = dma("gpsimd", auv[:], uat_h[:], "wu0")
        res_read(("arena",), "mm", md_mark[NT - 2])
        prev_md_last = md_mark[NT - 2]

    for st, v in stream_cnt.items():
        wait(stream_eng[st], st, v)

    with ExitStack() as ctx:
        for s in list(CSEMS) + sorted(stream_cnt):
            sems[s] = ctx.enter_context(nc.semaphore(f"sem_{s}"))
        block = ctx.enter_context(nc.Block())

        @block.sync
        def _(e):
            for f in sch["sync"]:
                f(e)

        @block.gpsimd
        def _(e):
            for f in sch["gpsimd"]:
                f(e)

        @block.tensor
        def _(e):
            for f in sch["tensor"]:
                f(e)

        @block.vector
        def _(e):
            for f in sch["vector"]:
                f(e)

        @block.scalar
        def _(e):
            for f in sch["scalar"]:
                f(e)

    return nc


_NC_CACHE = []


def kernel(L0_attn_data, L0_attn_mask, L0_attn_weight, L0_attn_bias,
           L0_ffn_gate_q, L0_ffn_gate_scale, L0_ffn_gate_A, L0_ffn_gate_B,
           L0_ffn_up_q, L0_ffn_up_scale, L0_ffn_up_A, L0_ffn_up_B,
           L0_ffn_down_q, L0_ffn_down_scale, L0_ffn_down_A, L0_ffn_down_B):
    f32 = np.float32
    x = np.asarray(L0_attn_data, f32).reshape(8192, 4096)

    def tile_q(q, nt_o, nt_c):
        a = np.asarray(q, np.int32).reshape(nt_o, 128, nt_c, 128)
        a = np.ascontiguousarray(a.transpose(0, 3, 2, 1)).reshape(nt_o * 128, nt_c * 128)
        return np.ascontiguousarray(a.view(np.int8)[:, 0::4])

    gq = tile_q(L0_ffn_gate_q, IT, HT)
    uq = tile_q(L0_ffn_up_q, IT, HT)
    dq = tile_q(L0_ffn_down_q, NT, IT)
    wt = np.ascontiguousarray(
        np.asarray(L0_attn_weight, f32).reshape(NT, 128, HT, 128)
        .transpose(0, 3, 2, 1)).reshape(NT * 128, HT * 128)
    mk32 = np.ascontiguousarray(np.asarray(L0_attn_mask, np.int32))
    mk = np.ascontiguousarray(mk32.view(np.int8)[:, 0::4])

    def panel(sc, nt):
        return np.ascontiguousarray(np.asarray(sc, f32).reshape(nt, 128).T)

    def tile_a(A, nt):
        a = np.asarray(A, f32).reshape(nt, 128, R)
        return np.ascontiguousarray(a.transpose(1, 0, 2)).reshape(128, nt * R)

    common = {
        "gq": gq, "uq": uq, "dq": dq, "wt": wt, "mk": mk,
        "gscr": np.ascontiguousarray(np.asarray(L0_ffn_gate_scale, f32).reshape(1, II)),
        "uscr": np.ascontiguousarray(np.asarray(L0_ffn_up_scale, f32).reshape(1, II)),
        "dscr": np.ascontiguousarray(np.asarray(L0_ffn_down_scale, f32).reshape(1, H)),
        "gscp": panel(L0_ffn_gate_scale, IT),
        "uscp": panel(L0_ffn_up_scale, IT),
        "dscp": panel(L0_ffn_down_scale, NT),
        "gat": tile_a(L0_ffn_gate_A, HT),
        "uat": tile_a(L0_ffn_up_A, HT),
        "dat": tile_a(L0_ffn_down_A, IT),
        "gb": np.ascontiguousarray(np.asarray(L0_ffn_gate_B, f32)),
        "ub": np.ascontiguousarray(np.asarray(L0_ffn_up_B, f32)),
        "db": np.ascontiguousarray(np.asarray(L0_ffn_down_B, f32)),
        "biasp": panel(L0_attn_bias, NT),
    }
    in_maps = []
    for c in range(8):
        m = dict(common)
        m["xt"] = np.ascontiguousarray(x[c * 1024:(c + 1) * 1024, :].T)
        in_maps.append(m)

    if not _NC_CACHE:
        _NC_CACHE.append(build_program())
    nc = _NC_CACHE[0]
    res = run_bass_kernel_spmd(nc, in_maps, core_ids=list(range(8)))
    parts = [res.results[c]["out"] for c in range(8)]
    big = np.concatenate(parts, axis=1)
    return np.ascontiguousarray(big.T).reshape(1, 8192, 4096)


# version 22
# speedup vs baseline: 1.0174x; 1.0174x over previous
import numpy as np
import ml_dtypes
import concourse.bass as bass
import concourse.mybir as mybir
from concourse.bass_utils import run_bass_kernel_spmd
from contextlib import ExitStack

F32 = mybir.dt.float32
BF16 = mybir.dt.bfloat16
I32 = mybir.dt.int32
I8 = mybir.dt.int8
AF = mybir.ActivationFunctionType
ALU = mybir.AluOpType
BF = ml_dtypes.bfloat16

H = 4096
HT = 32
II = 11008
IT = 86
NT = 32
S = 512
CH = 2
R = 32

ENGS = ("sync", "gpsimd", "tensor", "vector", "scalar")
CSEMS = ("mm", "ve", "ac")


def build_program():
    nc = bass.Bass("TRN2", target_bir_lowering=False)

    ap = lambda t: t.ap()
    xt_h = ap(nc.dram_tensor("xt", [H, CH * S], F32, kind="ExternalInput"))
    gq_h = ap(nc.dram_tensor("gq", [IT * 128, HT * 128], I8, kind="ExternalInput"))
    uq_h = ap(nc.dram_tensor("uq", [IT * 128, HT * 128], I8, kind="ExternalInput"))
    dq_h = ap(nc.dram_tensor("dq", [NT * 128, IT * 128], I8, kind="ExternalInput"))
    wt_h = ap(nc.dram_tensor("wt", [NT * 128, HT * 128], F32, kind="ExternalInput"))
    mk_h = ap(nc.dram_tensor("mk", [H, H], I8, kind="ExternalInput"))
    gscr_h = ap(nc.dram_tensor("gscr", [1, II], F32, kind="ExternalInput"))
    uscr_h = ap(nc.dram_tensor("uscr", [1, II], F32, kind="ExternalInput"))
    dscr_h = ap(nc.dram_tensor("dscr", [1, H], F32, kind="ExternalInput"))
    gscp_h = ap(nc.dram_tensor("gscp", [128, IT], F32, kind="ExternalInput"))
    uscp_h = ap(nc.dram_tensor("uscp", [128, IT], F32, kind="ExternalInput"))
    dscp_h = ap(nc.dram_tensor("dscp", [128, NT], F32, kind="ExternalInput"))
    gat_h = ap(nc.dram_tensor("gat", [128, HT * R], F32, kind="ExternalInput"))
    uat_h = ap(nc.dram_tensor("uat", [128, HT * R], F32, kind="ExternalInput"))
    dat_h = ap(nc.dram_tensor("dat", [128, IT * R], F32, kind="ExternalInput"))
    gb_h = ap(nc.dram_tensor("gb", [R, II], F32, kind="ExternalInput"))
    ub_h = ap(nc.dram_tensor("ub", [R, II], F32, kind="ExternalInput"))
    db_h = ap(nc.dram_tensor("db", [R, H], F32, kind="ExternalInput"))
    bias_h = ap(nc.dram_tensor("biasp", [128, NT], F32, kind="ExternalInput"))
    out_h = ap(nc.dram_tensor("out", [H, CH * S], F32, kind="ExternalOutput"))
    bpg_h = ap(nc.dram_tensor("bpg_s", [IT * R, 128], BF16, kind="Internal"))
    bpu_h = ap(nc.dram_tensor("bpu_s", [IT * R, 128], BF16, kind="Internal"))
    bpd_h = ap(nc.dram_tensor("bpd_s", [NT * R, 128], BF16, kind="Internal"))
    wms_h = ap(nc.dram_tensor("wm_s", [NT * 128, HT * 128], BF16, kind="Internal"))

    p_idx = np.arange(128)
    t_idx = np.arange(32)
    rexp = (p_idx[:, None, None] == (4 * t_idx[None, :, None] + (p_idx[None, None, :] // 32)))
    rexp_c = ap(nc.inline_tensor(np.ascontiguousarray(rexp.reshape(128, 4096).astype(BF)), "rexp_c"))
    i128_c = ap(nc.inline_tensor(np.eye(128, dtype=BF), "i128_c"))
    c75_c = ap(nc.inline_tensor(np.full((1, 128), -7.5, BF), "c75_c"))
    onesc_c = ap(nc.inline_tensor(np.ones((128, 1), BF), "onesc_c"))
    ones1_c = ap(nc.inline_tensor(np.ones((1, 128), np.float32), "ones1_c"))
    ones32_c = ap(nc.inline_tensor(np.ones((1, 32), np.float32), "ones32_c"))
    eps_c = ap(nc.inline_tensor(np.full((1, 1), 1e-5, np.float32), "eps_c"))
    lnc_c = ap(nc.inline_tensor(np.full((1, 1), np.log(np.float32(0.02)), np.float32),
                                "lnc_c"))

    BASE = 16448

    def sb(name, shape, dtype, off):
        return nc.alloc_sbuf_tensor_at(name, shape, dtype, offset=BASE + off)

    xn_t = sb("xn", [128, HT * S], BF16, 0)
    h_t = sb("h", [128, IT * S], BF16, 32768)
    s_t = sb("sres", [128, NT * S], BF16, 120832)
    AR = 153600
    agv = sb("agv", [128, HT * R], BF16, AR)
    auv = sb("auv", [128, HT * R], BF16, AR + 2048)
    wgv = [sb("wgv0", [128, HT * 128], BF16, AR),
           sb("wgv1", [128, HT * 128], BF16, AR + 8192)]
    wuv = [sb("wuv0", [128, HT * 128], BF16, AR + 16384),
           sb("wuv1", [128, HT * 128], BF16, AR + 24576)]
    wav = [sb("wav0", [128, HT * 128], BF16, AR),
           sb("wav1", [128, HT * 128], BF16, AR + 8192)]
    wdv = [sb("wdv0", [128, IT * 128], BF16, AR),
           sb("wdv1", [128, IT * 128], BF16, 0)]
    wdv_s = sb("wdv_s", [128, IT * 128], BF16, 120832)
    a_g = sb("a_g", [128, IT], F32, 186368)
    a_u = sb("a_u", [128, IT], F32, 186720)
    a_d = sb("a_d", [128, NT], F32, 187072)
    biasp = sb("biasv", [128, NT], F32, 187200)
    t_ar = sb("t_ar", [128, 3 * S], BF16, 187328)
    bp_bb = sb("bp_bb", [128, 768], BF16, 190400)
    adb = sb("adb", [128, 66], BF16, 191936)
    m_sb = sb("m_sb", [1, S], F32, 192256)
    inv_sb = sb("inv_sb", [1, S], F32, 194304)
    c75_sb = sb("c75_sb", [1, 128], BF16, 196352)
    onesc_sb = sb("onesc_sb", [128, 1], BF16, 196608)
    ones1_sb = sb("ones1_sb", [1, 128], F32, 196672)
    ones32_sb = sb("ones32_sb", [1, 32], F32, 197184)
    xst = sb("xst", [128, 2 * S], F32, 197312)
    ost = sb("ost", [128, 2 * S], F32, 201408)
    xo4 = sb("xo4", [128, 4 * S], F32, 197312)
    sgst = sb("sgst", [128, 2 * S], BF16, 207552)
    eps_sb = sb("eps_sb", [1, 1], F32, 209600)
    lnc_sb = sb("lnc_sb", [1, 1], F32, 209632)
    SR = 120832
    gbf_sb = sb("gbf_sb", [R, II], F32, 32768)
    iar_sb = sb("iar_sb", [1, II], F32, 76800)
    bst_sb = sb("bst_sb", [R, 2 * S], BF16, 205504)
    wf32 = sb("wf32", [128, 4096], F32, SR)
    rexpf = sb("rexpf", [128, 4096], BF16, SR)
    accm = sb("accm", [128, 4096], I8, SR + 16384)
    bmq_t = sb("bmq", [128, NT * 128], BF16, SR + 16384)
    wtmp = sb("wtmp", [128, 4096], BF16, SR + 24576)
    bm01_sb = sb("bm01_sb", [128, 128], BF16, 209664)
    bmn_sb = sb("bmn_sb", [128, 128], BF16, 209920)
    i128b = sb("i128b", [128, 128], BF16, 210176)

    var_v = ost[0:1, 0:S]

    ps = [nc.alloc_psum_tensor(f"ps{k}", [128, S], F32) for k in range(8)]

    sch = {k: [] for k in ENGS}
    cnt = dict.fromkeys(CSEMS, 0)
    stream_cnt = {}
    stream_eng = {}
    sems = {}

    def em(eng, f):
        sch[eng].append(f)

    sat = {}

    def wait(eng, k, v):
        if v and v > 0 and sat.get((eng, k), 0) < v:
            sat[(eng, k)] = v
            em(eng, lambda e, k=k, v=v: e.wait_ge(sems[k], v))

    def waits(eng, ws):
        d = {}
        for k, v in ws:
            if v and v > 0:
                d[k] = max(d.get(k, 0), v)
        for k, v in d.items():
            wait(eng, k, v)

    def dma(eng, out_ap, in_ap, st, accum=False):
        assert stream_eng.setdefault(st, eng) == eng, st
        prev = stream_cnt.get(st, 0)
        if prev >= 32:
            em(eng, lambda e, st=st, v=prev - 16: e.wait_ge(sems[st], v))
        stream_cnt[st] = prev + 16
        v = stream_cnt[st]
        if accum:
            em(eng, lambda e, o=out_ap, i=in_ap, st=st:
               e.dma_start(out=o, in_=i, accum_op=ALU.add).then_inc(sems[st], 16))
        else:
            em(eng, lambda e, o=out_ap, i=in_ap, st=st:
               e.dma_start(out=o, in_=i).then_inc(sems[st], 16))
        return (st, v)

    def mm(out_ap, lhsT, rhs, start, stop, inc=False):
        if inc:
            cnt["mm"] += 1
            v = cnt["mm"]
            em("tensor", lambda e, o=out_ap, l=lhsT, r=rhs, st=start, sp=stop, v=v:
               e.matmul(out=o, lhsT=l, rhs=r, start=st, stop=sp).then_inc(sems["mm"], 1))
            return v
        em("tensor", lambda e, o=out_ap, l=lhsT, r=rhs, st=start, sp=stop:
           e.matmul(out=o, lhsT=l, rhs=r, start=st, stop=sp))
        return None

    def act(fn_builder, inc=True):
        if inc:
            cnt["ac"] += 1
            v = cnt["ac"]
            em("scalar", lambda e, f=fn_builder, v=v: f(e).then_inc(sems["ac"], 1))
            return v
        em("scalar", lambda e, f=fn_builder: f(e))
        return None

    def vec(fn_builder, inc=True):
        if inc:
            cnt["ve"] += 1
            v = cnt["ve"]
            em("vector", lambda e, f=fn_builder, v=v: f(e).then_inc(sems["ve"], 1))
            return v
        em("vector", lambda e, f=fn_builder: f(e))
        return None

    bank_rd = [[] for _ in range(8)]

    def bank_write(bk):
        ws, bank_rd[bk] = bank_rd[bk], []
        waits("tensor", ws)

    res_rd = {}

    def res_write(eng, key):
        waits(eng, res_rd.pop(key, []))

    def res_read(key, k, v):
        res_rd.setdefault(key, []).append((k, v))

    for dst, src in ((c75_sb[:], c75_c[:]),
                     (onesc_sb[:], onesc_c[:]), (ones1_sb[:], ones1_c[:]),
                     (ones32_sb[:], ones32_c[:]), (eps_sb[:], eps_c[:]),
                     (lnc_sb[:], lnc_c[:]),
                     (biasp[:], bias_h[:])):
        dma("sync", dst, src, "cst")
    dma("sync", a_g[:], gscp_h[:], "cst")
    dma("sync", a_u[:], uscp_h[:], "cst")
    m_ad = dma("sync", a_d[:], dscp_h[:], "cst")
    consts_mark = m_ad
    waits("scalar", [m_ad])
    act(lambda e: e.activation(out=a_g[:], in_=a_g[:], func=AF.Identity, scale=0.005))
    act(lambda e: e.activation(out=a_u[:], in_=a_u[:], func=AF.Identity, scale=0.005))
    act(lambda e: e.activation(out=a_d[:], in_=a_d[:], func=AF.Identity, scale=0.005))
    waits("vector", [consts_mark])
    for slot in range(6):
        vec(lambda e, slot=slot: e.tensor_copy(
            out=bp_bb[32:33, slot * 128:(slot + 1) * 128], in_=c75_sb[:]), inc=False)
    vec(lambda e: e.tensor_copy(out=adb[:, 32:33], in_=onesc_sb[:]), inc=False)
    vec(lambda e: e.tensor_copy(out=adb[:, 65:66], in_=onesc_sb[:]))

    waits("tensor", [consts_mark])

    def prep_bprime(scr_row, b_full, dst, n_out, stores):
        noc = (n_out + S - 1) // S
        state = {"pend": None}

        def flush():
            if state["pend"] is None:
                return
            oc, cw, bk, mmv, mb = state["pend"]
            state["pend"] = None
            waits("vector", [("mm", mmv), mb])
            res_write("vector", ("bst", bk))
            vv = vec(lambda e, oc=oc, cw=cw, bk=bk: e.tensor_tensor(
                out=bst_sb[0:R, bk * S:bk * S + cw],
                in0=gbf_sb[0:R, oc * S:oc * S + cw],
                in1=ps[bk][0:R, 0:cw], op=ALU.mult))
            bank_rd[bk].append(("ve", vv))
            res_read(("gbf",), "ve", vv)
            wait("sync", "ve", vv)
            nblk = cw // 128
            dst_ap = dst[oc * 4 * R:(oc * 4 + nblk) * R, :].rearrange(
                "(ot r) o -> r ot o", r=R)
            src_ap = bst_sb[0:R, bk * S:bk * S + cw].rearrange(
                "r (ot o) -> r ot o", o=128)
            sv = dma("sync", dst_ap, src_ap, "cst")
            res_read(("bst", bk), *sv)
            stores.append(sv)

        def step(oc):
            if oc == noc:
                flush()
                return
            if oc == 0:
                res_write("sync", ("iar",))
                state["m1"] = dma("sync", iar_sb[0:1, 0:n_out], scr_row, "cst")
                res_write("sync", ("gbf",))
                state["mb"] = dma("sync", gbf_sb[:, 0:n_out], b_full, "cst")
            m1, mb = state["m1"], state["mb"]
            cw = min(S, n_out - oc * S)
            bk = oc % 2
            row = iar_sb[0:1, oc * S:oc * S + cw]
            waits("scalar", [m1])
            act(lambda e, row=row: e.activation(out=row, in_=row,
                                                func=AF.Ln, scale=0.005), inc=False)
            v4 = act(lambda e, row=row: e.activation(out=row, in_=row, func=AF.Exp,
                                                     scale=-1.0, bias=lnc_sb[:]))
            wait("vector", "ac", v4)
            vmin = vec(lambda e, row=row: e.tensor_scalar_min(out=row, in0=row,
                                                              scalar1=1e30))
            bank_write(bk)
            waits("tensor", [("ve", vmin), mb])
            mmv = mm(ps[bk][0:R, 0:cw], ones32_sb[:], row, True, True, inc=True)
            res_read(("iar",), "mm", mmv)
            flush()
            state["pend"] = (oc, cw, bk, mmv, mb)

        return [lambda oc=oc: step(oc) for oc in range(noc + 1)]

    g_stores = []
    u_stores = []
    d_stores = []
    bsteps = (prep_bprime(gscr_h[:], gb_h[:], bpg_h, II, g_stores)
              + prep_bprime(uscr_h[:], ub_h[:], bpu_h, II, u_stores)
              + prep_bprime(dscr_h[:], db_h[:], bpd_h, H, d_stores))

    mkv = mk_h[:, :].rearrange("(nb r) k -> r nb k", r=32)
    ch_prev = {}

    def chain_issue(i):
        r, c = i // 4, i % 4
        st = f"ma{c}"
        if c in ch_prev:
            wait("gpsimd", *ch_prev[c])
        cs = slice(c * 1024, (c + 1) * 1024)
        ch_prev[c] = dma("gpsimd", accm[:, cs], mkv[r][:, cs], st, accum=(r > 0))

    xbl = {}
    for ht in range(2):
        xbl[ht] = dma("gpsimd", xn_t[:, ht * S:(ht + 1) * S],
                      xt_h[ht * 128:(ht + 1) * 128, 0:S], f"xb{ht}")
    pre_gu = {
        "wg1": dma("gpsimd", wgv[1][:], gq_h[128:256, :], "wg1"),
        "wu0": dma("gpsimd", wuv[0][:], uq_h[0:128, :], "wu0"),
        "wu1": dma("gpsimd", wuv[1][:], uq_h[128:256, :], "wu1"),
        "ad0": dma("gpsimd", adb[:, 0:32], dat_h[:, 0:R], "ad0"),
        "ad1": dma("gpsimd", adb[:, 33:65], dat_h[:, R:2 * R], "ad1"),
    }
    for ht in range(2, HT):
        xbl[ht] = dma("gpsimd", xn_t[:, ht * S:(ht + 1) * S],
                      xt_h[ht * 128:(ht + 1) * 128, 0:S], f"xb{ht % 8}")
    i128_ld = dma("sync", i128b[:], i128_c[:], "cst")
    rexp_ld = dma("sync", rexpf[:], rexp_c[:], "cst")
    for i in range(128):
        chain_issue(i)
    mload = {}
    wmst = {}
    wm_done = [None, None]
    bmark = {}

    def burst_step(ot):
        if ot < 8:
            pass
        elif ot == 8:
            waits("vector", [ch_prev[c] for c in range(4)])

            def _red(e):
                with nc.allow_low_precision(reason="0/1 mask any-nonzero; sign-exact"):
                    return e.tensor_reduce(
                        out=bm01_sb[:], in_=accm[:].rearrange("p (kb k) -> p kb k", k=32),
                        axis=mybir.AxisListType.X, op=ALU.add)

            vred = vec(_red)
            res_read(("accm",), "ve", vred)
            wait("vector", "ve", vred)
            bmark["vbm"] = vec(lambda e: e.tensor_single_scalar(
                out=bm01_sb[:], in_=bm01_sb[:], scalar=0.0, op=ALU.is_gt))
        elif ot == 9:
            bank_write(6)
            waits("tensor", [("ve", bmark["vbm"]), i128_ld])
            mbn = mm(ps[6][:, 0:128], bm01_sb[:], i128b[:], True, True, inc=True)
            waits("scalar", [("mm", mbn)])
            bmark["abn"] = act(lambda e: e.activation(out=bmn_sb[:], in_=ps[6][:, 0:128],
                                                      func=AF.Copy))
            bank_rd[6].append(("ac", bmark["abn"]))
        elif 10 <= ot < 18:
            for j in range(4):
                kt = 4 * (ot - 10) + j
                bk = 6 + kt % 2
                bank_write(bk)
                waits("tensor", [("ac", bmark["abn"])] + ([rexp_ld] if kt == 0 else []))
                mq = mm(ps[bk][:, 0:128], rexpf[:, kt * 128:(kt + 1) * 128],
                        bmn_sb[:], True, True, inc=True)
                res_read(("rexpf",), "mm", mq)
                waits("scalar", [("mm", mq)])
                if kt == 0:
                    res_write("scalar", ("accm",))
                aq = act(lambda e, kt=kt, bk=bk: e.activation(
                    out=bmq_t[:, kt * 128:(kt + 1) * 128], in_=ps[bk][:, 0:128],
                    func=AF.Copy))
                bank_rd[bk].append(("ac", aq))
                res_read(("bmq",), "ac", aq)
                if kt == NT - 1:
                    wm_done[0] = ("ac", cnt["ac"])
        elif ot >= 18 and bmark.get("wc", 0) < NT and ot % 2 == 0:
            t = bmark.get("wc", 0)
            bmark["wc"] = t + 1
            wstep(t)

    def wstep(t):
        if t == 0:
            res_write("scalar", ("rexpf",))
            wml = dma("scalar", wf32[:], wt_h[0:128, :], "wf0")
            mload[NT] = wml
        else:
            wml = mload[NT + t]
        waits("vector", [wml, wm_done[0]])
        res_write("vector", ("wtmp",))
        vec(lambda e: e.tensor_copy(out=wtmp[:], in_=wf32[:]), inc=False)
        res_read(("wf32",), "ve", cnt["ve"] + 1)
        if t + 1 < NT:
            eng2 = "scalar" if (t + 1) % 2 == 0 else "sync"
            res_write(eng2, ("wf32",))
            mload[NT + t + 1] = dma(eng2, wf32[:],
                                    wt_h[(t + 1) * 128:(t + 2) * 128, :],
                                    f"wf{(t + 1) % 2}")
        bmq_v = bmq_t[:].rearrange("p (kt nbg) -> p kt nbg", kt=NT)[
            :, :, 4 * t:4 * t + 4].unsqueeze(3).broadcast_to([128, NT, 4, 32])
        wmv = vec(lambda e, bmq_v=bmq_v: e.tensor_tensor(
            out=wtmp[:].rearrange("p (kt nb r) -> p kt nb r", kt=NT, nb=4),
            in0=wtmp[:].rearrange("p (kt nb r) -> p kt nb r", kt=NT, nb=4),
            in1=bmq_v, op=ALU.mult))
        res_read(("bmq",), "ve", wmv)
        wait("sync", "ve", wmv)
        wmst[t] = dma("sync", wms_h[t * 128:(t + 1) * 128, :], wtmp[:], "cst")
        res_read(("wtmp",), *wmst[t])
        if t == NT - 1:
            wm_done[1] = wmst[t]

    prev_md_last = None
    t_pre = {}
    for u in range(CH):
        tok = slice(u * S, (u + 1) * S)

        xq_mark = {}
        for ht in range(HT):
            par = ht % 2
            waits("scalar", [xbl[ht]])
            res_write("scalar", ("sgst", par))
            xq_mark[ht] = act(lambda e, ht=ht, par=par: e.activation(
                out=sgst[:, par * S:(par + 1) * S], in_=xn_t[:, ht * S:(ht + 1) * S],
                func=AF.Square))
            if ht == 0:
                bank_write(6)
                bank_write(7)
            waits("tensor", [("ac", xq_mark[ht]), xbl[ht]])
            mm(ps[6][0:1, :], onesc_sb[:], xn_t[:, ht * S:(ht + 1) * S],
               start=(ht == 0), stop=(ht == HT - 1), inc=False)
            mm(ps[7][0:1, :], onesc_sb[:], sgst[:, par * S:(par + 1) * S],
               start=(ht == 0), stop=(ht == HT - 1), inc=True)
            ht_mark = cnt["mm"]
            res_read(("sgst", par), "mm", ht_mark)
            if u == 0:
                if bsteps:
                    bsteps.pop(0)()
                if bsteps:
                    bsteps.pop(0)()
        stats_mark = cnt["mm"]
        waits("scalar", [("mm", stats_mark)])
        am = act(lambda e: e.activation(out=m_sb[:], in_=ps[6][0:1, :],
                                        func=AF.Identity, scale=1.0 / H))
        bank_rd[6].append(("ac", am))
        waits("vector", [("ac", am)])
        vec(lambda e: e.tensor_tensor(out=inv_sb[:], in0=m_sb[:], in1=m_sb[:],
                                      op=ALU.mult), inc=False)
        waits("vector", [("mm", stats_mark)])
        res_write("vector", ("ost", 0))
        vvar = vec(lambda e: e.scalar_tensor_tensor(
            out=var_v, in0=ps[7][0:1, :], scalar=1.0 / H, in1=inv_sb[:],
            op0=ALU.mult, op1=ALU.subtract))
        bank_rd[7].append(("ve", vvar))
        waits("scalar", [("ve", vvar)])
        act(lambda e: e.activation(out=var_v, in_=var_v, func=AF.Ln, bias=eps_sb[:]),
            inc=False)
        ainv = act(lambda e: e.activation(out=inv_sb[:], in_=var_v, func=AF.Exp,
                                          scale=-0.5))
        bank_write(2)
        waits("tensor", [("ac", am)])
        mm(ps[2][:, :], ones1_sb[:], m_sb[:], True, True, inc=False)
        bank_write(3)
        waits("tensor", [("ac", ainv)])
        mib = mm(ps[3][:, :], ones1_sb[:], inv_sb[:], True, True, inc=True)
        res_read(("ost", 0), "mm", mib)
        xn_mark = {}
        NG = HT // 4
        ps2b = ps[2][:, :].unsqueeze(1).broadcast_to([128, 4, S])
        ps3b = ps[3][:, :].unsqueeze(1).broadcast_to([128, 4, S])
        for g in range(NG):
            t0 = 4 * g * S
            if g == 0:
                waits("vector", [("mm", mib)])
                for key in (("ost", 0), ("ost", 1), ("xst", 0), ("xst", 1)):
                    res_write("vector", key)
            vec(lambda e, t0=t0: e.tensor_tensor(
                out=xo4[:].rearrange("p (t s) -> p t s", s=S),
                in0=xn_t[:, t0:t0 + 4 * S].rearrange("p (t s) -> p t s", s=S),
                in1=ps2b, op=ALU.subtract), inc=False)
            xn_mark[g] = vec(lambda e, t0=t0: e.tensor_tensor(
                out=xn_t[:, t0:t0 + 4 * S].rearrange("p (t s) -> p t s", s=S),
                in0=xo4[:].rearrange("p (t s) -> p t s", s=S),
                in1=ps3b, op=ALU.mult))
            if g == 0:
                bank_write(6)
            waits("tensor", [("ve", xn_mark[g])])
            for t4 in range(4):
                ht = 4 * g + t4
                mm(ps[6][0:1, :], onesc_sb[:], xn_t[:, ht * S:(ht + 1) * S],
                   start=(ht == 0), stop=(ht == HT - 1), inc=(ht == HT - 1))
        rx_group = cnt["mm"]
        for key in (("ost", 0), ("ost", 1), ("xst", 0), ("xst", 1)):
            res_read(key, "ve", xn_mark[NG - 1])
        bank_rd[2].append(("ve", xn_mark[NG - 1]))
        bank_rd[3].append(("ve", xn_mark[NG - 1]))
        waits("scalar", [("mm", rx_group)])
        act(lambda e: e.activation(out=t_ar[32:33, 0:S], in_=ps[6][0:1, :],
                                   func=AF.Copy), inc=False)
        arx = act(lambda e: e.activation(out=t_ar[32:33, S:2 * S], in_=ps[6][0:1, :],
                                         func=AF.Copy))
        bank_rd[6].append(("ac", arx))
        if u == 0:
            while bsteps:
                bsteps.pop(0)()

        res_write("gpsimd", ("arena",))
        if u == 0:
            tga = dma("gpsimd", agv[:], gat_h[:], "wg0")
            tua = dma("gpsimd", auv[:], uat_h[:], "wu0")
        else:
            tga, tua = t_pre["g"], t_pre["u"]
        bank_write(4)
        waits("tensor", [tga])
        for ht in range(HT):
            mm(ps[4][0:R, :], agv[:, ht * R:(ht + 1) * R], xn_t[:, ht * S:(ht + 1) * S],
               start=(ht == 0), stop=(ht == HT - 1), inc=(ht == HT - 1))
        tg_mark = cnt["mm"]
        bank_write(5)
        waits("tensor", [tua])
        for ht in range(HT):
            mm(ps[5][0:R, :], auv[:, ht * R:(ht + 1) * R], xn_t[:, ht * S:(ht + 1) * S],
               start=(ht == 0), stop=(ht == HT - 1), inc=(ht == HT - 1))
        tu_mark = cnt["mm"]
        waits("scalar", [("mm", tg_mark)])
        atg = act(lambda e: e.activation(out=t_ar[0:R, 0:S], in_=ps[4][0:R, :], func=AF.Copy))
        bank_rd[4].append(("ac", atg))
        waits("scalar", [("mm", tu_mark)])
        atu = act(lambda e: e.activation(out=t_ar[0:R, S:2 * S], in_=ps[5][0:R, :], func=AF.Copy))
        bank_rd[5].append(("ac", atu))

        wg_mark = {}
        wu_mark = {}
        bp_mark = {}
        ad_mark = {}

        def load_gu(ot, first=False):
            par = ot % 2
            if first:
                waits("gpsimd", [("mm", tu_mark)])
            res_write("gpsimd", ("wg", par))
            wg_mark[ot] = dma("gpsimd", wgv[par][:],
                              gq_h[ot * 128:(ot + 1) * 128, :], f"wg{par}")
            res_write("gpsimd", ("wu", par))
            wu_mark[ot] = dma("gpsimd", wuv[par][:],
                              uq_h[ot * 128:(ot + 1) * 128, :], f"wu{par}")

        def load_bp(ot):
            par = ot % 2
            res_write("sync", ("bpg", par))
            waits("sync", [g_stores[min(ot // 4, len(g_stores) - 1)]])
            dma("sync", bp_bb[0:R, par * 128:(par + 1) * 128],
                bpg_h[ot * R:(ot + 1) * R, :], f"bp{par}")
            res_write("sync", ("bpu", par))
            waits("sync", [u_stores[min(ot // 4, len(u_stores) - 1)]])
            bp_mark[ot] = dma("sync", bp_bb[0:R, 256 + par * 128:256 + (par + 1) * 128],
                              bpu_h[ot * R:(ot + 1) * R, :], f"bp{par}")

        def load_ad(ot):
            par = ot % 2
            res_write("gpsimd", ("adb", par))
            ad_mark[ot] = dma("gpsimd", adb[:, par * 33:par * 33 + 32],
                              dat_h[:, ot * R:(ot + 1) * R], f"ad{par}")

        if u == 0:
            waits("gpsimd", [("mm", tu_mark)])
            res_write("gpsimd", ("wg", 0))
            wg_mark[0] = dma("gpsimd", wgv[0][:], gq_h[0:128, :], "wg0")
            wg_mark[1] = pre_gu["wg1"]
            wu_mark[0] = pre_gu["wu0"]
            wu_mark[1] = pre_gu["wu1"]
            ad_mark[0] = pre_gu["ad0"]
            ad_mark[1] = pre_gu["ad1"]
        else:
            load_gu(0, first=True)
            load_gu(1, first=True)
            load_ad(0)
            load_ad(1)
        load_bp(0)
        load_bp(1)
        mg_mark = {}
        mu_mark = {}
        mtd_mark = {}
        h_mark = {}
        sg_mark = {}
        for ot in range(IT):
            par = ot % 2
            gbk, ubk = par, 2 + par
            bank_write(gbk)
            waits("tensor", [("ac", arx), bp_mark[ot], ("ac", atg)])
            mm(ps[gbk][:, :], bp_bb[0:33, par * 128:(par + 1) * 128],
               t_ar[0:33, 0:S], True, False)
            waits("tensor", [wg_mark[ot]])
            for ht in range(HT):
                mm(ps[gbk][:, :], wgv[par][:, ht * 128:(ht + 1) * 128],
                   xn_t[:, ht * S:(ht + 1) * S], False, (ht == HT - 1), inc=(ht == HT - 1))
            mg_mark[ot] = cnt["mm"]
            res_read(("wg", par), "mm", mg_mark[ot])
            res_read(("bpg", par), "mm", mg_mark[ot])
            res_read(("bpu", par), "mm", mg_mark[ot])
            bank_write(ubk)
            waits("tensor", [("ac", atu)])
            mm(ps[ubk][:, :], bp_bb[0:33, 256 + par * 128:256 + (par + 1) * 128],
               t_ar[0:33, S:2 * S], True, False)
            waits("tensor", [wu_mark[ot]])
            for ht in range(HT):
                mm(ps[ubk][:, :], wuv[par][:, ht * 128:(ht + 1) * 128],
                   xn_t[:, ht * S:(ht + 1) * S], False, (ht == HT - 1), inc=(ht == HT - 1))
            mu_mark[ot] = cnt["mm"]
            res_read(("wu", par), "mm", mu_mark[ot])
            if ot > 0:
                po = ot - 1
                ppar = po % 2
                if po == 0:
                    bank_write(4)
                    waits("tensor", [("ac", atg)])
                waits("tensor", [("ve", h_mark[po]), ad_mark[po]])
                mm(ps[4][0:33, :], adb[:, ppar * 33:(ppar + 1) * 33],
                   h_t[:, po * S:(po + 1) * S], start=(po == 0), stop=(po == IT - 1),
                   inc=True)
                mtd_mark[po] = cnt["mm"]
                res_read(("adb", ppar), "mm", mtd_mark[po])
                if po + 2 < IT:
                    load_ad(po + 2)
            waits("scalar", [("mm", mg_mark[ot])])
            res_write("scalar", ("sgst", par))
            sg_mark[ot] = act(lambda e, ot=ot, par=par, gbk=gbk: e.activation(
                out=sgst[:, par * S:(par + 1) * S], in_=ps[gbk][:, :], func=AF.Silu,
                scale=a_g[:, ot:ot + 1]))
            bank_rd[gbk].append(("ac", sg_mark[ot]))
            waits("vector", [("ac", sg_mark[ot]), ("mm", mu_mark[ot])])
            h_mark[ot] = vec(lambda e, ot=ot, par=par, ubk=ubk: e.scalar_tensor_tensor(
                out=h_t[:, ot * S:(ot + 1) * S], in0=ps[ubk][:, :],
                scalar=a_u[:, ot:ot + 1], in1=sgst[:, par * S:(par + 1) * S],
                op0=ALU.mult, op1=ALU.mult))
            bank_rd[ubk].append(("ve", h_mark[ot]))
            res_read(("sgst", par), "ve", h_mark[ot])
            if ot + 2 < IT:
                load_bp(ot + 2)
                load_gu(ot + 2)
            if u == 0:
                burst_step(ot)
        po = IT - 1
        ppar = po % 2
        waits("tensor", [("ve", h_mark[po]), ad_mark[po]])
        mm(ps[4][0:33, :], adb[:, ppar * 33:(ppar + 1) * 33],
           h_t[:, po * S:(po + 1) * S], start=False, stop=True, inc=True)
        mtd_mark[po] = cnt["mm"]
        res_read(("adb", ppar), "mm", mtd_mark[po])
        waits("scalar", [("mm", mtd_mark[po])])
        atd = act(lambda e: e.activation(out=t_ar[0:33, 2 * S:3 * S], in_=ps[4][0:33, :],
                                         func=AF.Copy))
        bank_rd[4].append(("ac", atd))

        wa_mark = {}
        ms_mark = {}

        def load_wa(nt, extra=None):
            par = nt % 2
            eng = "sync" if par == 0 else "scalar"
            res_write(eng, ("wa", par))
            if extra:
                waits(eng, extra)
            if extra is None and u == 0:
                waits(eng, [wmst[nt]])
            wa_mark[nt] = dma(eng, wav[par][:],
                              wms_h[nt * 128:(nt + 1) * 128, :], f"wa{par}")

        wmg = [wmst[0]] if u == 0 else []
        load_wa(0, extra=[("mm", mg_mark[IT - 2])] + ([wmst[0]] if u == 0 else []))
        load_wa(1, extra=[("mm", mg_mark[IT - 1])] + ([wmst[1]] if u == 0 else []))
        for nt in range(NT):
            par = nt % 2
            sbk = par
            bank_write(sbk)
            waits("tensor", [wa_mark[nt]])
            for kt in range(HT):
                mm(ps[sbk][:, :], wav[par][:, kt * 128:(kt + 1) * 128],
                   xn_t[:, kt * S:(kt + 1) * S], start=(kt == 0), stop=(kt == HT - 1),
                   inc=(kt == HT - 1))
            ms_mark[nt] = cnt["mm"]
            res_read(("wa", par), "mm", ms_mark[nt])
            waits("scalar", [("mm", ms_mark[nt])])
            if u == 0 and nt == 0:
                waits("scalar", [wm_done[1], ("ac", wm_done[0][1])])
                for key in (("bmq",), ("wf32",), ("wtmp",), ("accm",), ("rexpf",)):
                    res_write("scalar", key)
            se = act(lambda e, nt=nt, sbk=sbk: e.activation(
                out=s_t[:, nt * S:(nt + 1) * S], in_=ps[sbk][:, :], func=AF.Identity,
                bias=biasp[:, nt:nt + 1]))
            bank_rd[sbk].append(("ac", se))
            if nt + 2 < NT:
                load_wa(nt + 2)

        wd_mark = {}
        md_mark = {}
        bpd_mark = {}
        gt_mark = {}
        cb_mark = {}
        xld2 = {}

        def wd_buf(ho):
            if ho % 2 == 1 and ho >= 25:
                return wdv_s, ("wds",)
            return wdv[ho % 2], ("wd", ho % 2)

        wd_parts = {}

        def load_wd(ho, extra=None):
            buf, key = wd_buf(ho)
            res_write("gpsimd", key)
            if ho == 25:
                waits("gpsimd", [("ve", cb_mark[21])])
            if extra:
                waits("gpsimd", extra)
            if ho == 0:
                for j in range(4):
                    c0, c1 = 2816 * j, min(2816 * (j + 1), IT * 128)
                    wd_parts[j] = dma("gpsimd", buf[:, c0:c1],
                                      dq_h[0:128, c0:c1], f"wd0l")
                wd_mark[ho] = wd_parts[3]
            else:
                wd_mark[ho] = dma("gpsimd", buf[:],
                                  dq_h[ho * 128:(ho + 1) * 128, :], f"wd{ho % 2}l")

        def load_bpd(ho):
            par = ho % 2
            res_write("sync", ("bpd", par))
            waits("sync", [d_stores[min(ho // 4, len(d_stores) - 1)]])
            bpd_mark[ho] = dma("sync", bp_bb[0:R, 512 + par * 128:512 + (par + 1) * 128],
                               bpd_h[ho * R:(ho + 1) * R, :], f"bp{par}")

        def load_x2(ho):
            par = ho % 2
            res_write("scalar", ("xst", par))
            xld2[ho] = dma("scalar", xst[:, par * S:(par + 1) * S],
                           xt_h[ho * 128:(ho + 1) * 128, tok], f"x{par}")

        load_wd(0, extra=[("mm", ms_mark[NT - 1])])
        load_wd(1, extra=[("mm", ms_mark[NT - 1])])
        load_bpd(0)
        load_bpd(1)
        load_x2(0)
        load_x2(1)
        for ho in range(NT):
            par = ho % 2
            dbk = 2 + par
            bank_write(dbk)
            waits("tensor", [("ac", atd), bpd_mark[ho]])
            mm(ps[dbk][:, :], bp_bb[0:33, 512 + par * 128:512 + (par + 1) * 128],
               t_ar[0:33, 2 * S:3 * S], True, False)
            buf, key = wd_buf(ho)
            if ho != 0:
                waits("tensor", [wd_mark[ho]])
            for it in range(IT):
                if ho == 0 and it % 22 == 0:
                    waits("tensor", [wd_parts[it // 22]])
                mm(ps[dbk][:, :], buf[:, it * 128:(it + 1) * 128],
                   h_t[:, it * S:(it + 1) * S], False, (it == IT - 1), inc=(it == IT - 1))
            md_mark[ho] = cnt["mm"]
            res_read(key, "mm", md_mark[ho])
            res_read(("bpd", par), "mm", md_mark[ho])
            waits("scalar", [("mm", md_mark[ho])])
            res_write("scalar", ("sgst", par))
            gt_mark[ho] = act(lambda e, ho=ho, par=par, dbk=dbk: e.activation(
                out=sgst[:, par * S:(par + 1) * S], in_=ps[dbk][:, :], func=AF.Silu,
                scale=a_d[:, ho:ho + 1]))
            bank_rd[dbk].append(("ac", gt_mark[ho]))
            waits("vector", [("ac", gt_mark[ho]), xld2[ho]])
            res_write("vector", ("ost", par))
            vec(lambda e, ho=ho, par=par: e.tensor_tensor(
                out=ost[:, par * S:(par + 1) * S], in0=s_t[:, ho * S:(ho + 1) * S],
                in1=sgst[:, par * S:(par + 1) * S], op=ALU.mult), inc=False)
            cb_mark[ho] = vec(lambda e, ho=ho, par=par: e.tensor_tensor(
                out=ost[:, par * S:(par + 1) * S], in0=ost[:, par * S:(par + 1) * S],
                in1=xst[:, par * S:(par + 1) * S], op=ALU.add))
            res_read(("sgst", par), "ve", cb_mark[ho])
            res_read(("xst", par), "ve", cb_mark[ho])
            waits("sync", [("ve", cb_mark[ho])])
            ov = dma("sync", out_h[ho * 128:(ho + 1) * 128, tok],
                     ost[:, par * S:(par + 1) * S], f"o{par}")
            res_read(("ost", par), *ov)
            if ho + 2 < NT:
                load_bpd(ho + 2)
                load_x2(ho + 2)
                load_wd(ho + 2)
            if u == 0 and ho % 2 == 1 and 23 <= ho <= 29:
                if ho == 23:
                    waits("gpsimd", [("mm", md_mark[23])])
                for ht2 in range(8 * ((ho - 23) // 2), 8 * ((ho - 23) // 2) + 8):
                    xbl[ht2] = dma("gpsimd", xn_t[:, ht2 * S:(ht2 + 1) * S],
                                   xt_h[ht2 * 128:(ht2 + 1) * 128, 512:1024],
                                   f"xb{ht2 % 4}")
            if u == 0 and ho == NT - 1:
                waits("gpsimd", [("mm", md_mark[NT - 2])])
                t_pre["g"] = dma("gpsimd", agv[:], gat_h[:], "wg0")
                t_pre["u"] = dma("gpsimd", auv[:], uat_h[:], "wu0")
        res_read(("arena",), "mm", md_mark[NT - 2])
        prev_md_last = md_mark[NT - 2]

    for st, v in stream_cnt.items():
        wait(stream_eng[st], st, v)

    with ExitStack() as ctx:
        for s in list(CSEMS) + sorted(stream_cnt):
            sems[s] = ctx.enter_context(nc.semaphore(f"sem_{s}"))
        block = ctx.enter_context(nc.Block())

        @block.sync
        def _(e):
            for f in sch["sync"]:
                f(e)

        @block.gpsimd
        def _(e):
            for f in sch["gpsimd"]:
                f(e)

        @block.tensor
        def _(e):
            for f in sch["tensor"]:
                f(e)

        @block.vector
        def _(e):
            for f in sch["vector"]:
                f(e)

        @block.scalar
        def _(e):
            for f in sch["scalar"]:
                f(e)

    return nc


_NC_CACHE = []


def kernel(L0_attn_data, L0_attn_mask, L0_attn_weight, L0_attn_bias,
           L0_ffn_gate_q, L0_ffn_gate_scale, L0_ffn_gate_A, L0_ffn_gate_B,
           L0_ffn_up_q, L0_ffn_up_scale, L0_ffn_up_A, L0_ffn_up_B,
           L0_ffn_down_q, L0_ffn_down_scale, L0_ffn_down_A, L0_ffn_down_B):
    f32 = np.float32
    x = np.asarray(L0_attn_data, f32).reshape(8192, 4096)

    def tile_q(q, nt_o, nt_c):
        a = np.asarray(q, np.int32).reshape(nt_o, 128, nt_c, 128)
        a = np.ascontiguousarray(a.transpose(0, 3, 2, 1)).reshape(nt_o * 128, nt_c * 128)
        return np.ascontiguousarray(a.view(np.int8)[:, 0::4])

    gq = tile_q(L0_ffn_gate_q, IT, HT)
    uq = tile_q(L0_ffn_up_q, IT, HT)
    dq = tile_q(L0_ffn_down_q, NT, IT)
    wt = np.ascontiguousarray(
        np.asarray(L0_attn_weight, f32).reshape(NT, 128, HT, 128)
        .transpose(0, 3, 2, 1)).reshape(NT * 128, HT * 128)
    mk32 = np.ascontiguousarray(np.asarray(L0_attn_mask, np.int32))
    mk = np.ascontiguousarray(mk32.view(np.int8)[:, 0::4])

    def panel(sc, nt):
        return np.ascontiguousarray(np.asarray(sc, f32).reshape(nt, 128).T)

    def tile_a(A, nt):
        a = np.asarray(A, f32).reshape(nt, 128, R)
        return np.ascontiguousarray(a.transpose(1, 0, 2)).reshape(128, nt * R)

    common = {
        "gq": gq, "uq": uq, "dq": dq, "wt": wt, "mk": mk,
        "gscr": np.ascontiguousarray(np.asarray(L0_ffn_gate_scale, f32).reshape(1, II)),
        "uscr": np.ascontiguousarray(np.asarray(L0_ffn_up_scale, f32).reshape(1, II)),
        "dscr": np.ascontiguousarray(np.asarray(L0_ffn_down_scale, f32).reshape(1, H)),
        "gscp": panel(L0_ffn_gate_scale, IT),
        "uscp": panel(L0_ffn_up_scale, IT),
        "dscp": panel(L0_ffn_down_scale, NT),
        "gat": tile_a(L0_ffn_gate_A, HT),
        "uat": tile_a(L0_ffn_up_A, HT),
        "dat": tile_a(L0_ffn_down_A, IT),
        "gb": np.ascontiguousarray(np.asarray(L0_ffn_gate_B, f32)),
        "ub": np.ascontiguousarray(np.asarray(L0_ffn_up_B, f32)),
        "db": np.ascontiguousarray(np.asarray(L0_ffn_down_B, f32)),
        "biasp": panel(L0_attn_bias, NT),
    }
    in_maps = []
    for c in range(8):
        m = dict(common)
        m["xt"] = np.ascontiguousarray(x[c * 1024:(c + 1) * 1024, :].T)
        in_maps.append(m)

    if not _NC_CACHE:
        _NC_CACHE.append(build_program())
    nc = _NC_CACHE[0]
    res = run_bass_kernel_spmd(nc, in_maps, core_ids=list(range(8)))
    parts = [res.results[c]["out"] for c in range(8)]
    big = np.concatenate(parts, axis=1)
    return np.ascontiguousarray(big.T).reshape(1, 8192, 4096)
